# version 02
# baseline (speedup 1.0000x reference)
import contextlib
import numpy as np
import concourse.bass as bass
import concourse.mybir as mybir
from concourse.bass_utils import run_bass_kernel_spmd

F32 = mybir.dt.float32
BF16 = mybir.dt.bfloat16
AF = mybir.ActivationFunctionType
ALU = mybir.AluOpType

NCORES = 8
D = 4096
SS = 1024
FF = 9984
NG = 6
KD = D // 128
KS = SS // 128
KF = FF // 128
NT = 2048
TS = 512
NTS = NT // TS
TH = NT // 2
THD = 2.5
EPS = 1e-5


class _Ring:
    def __init__(self, tiles, guard=()):
        self.tiles = list(tiles)
        self.free = [dict(guard) for _ in self.tiles]
        self.i = 0

    def acquire(self):
        i = self.i
        self.i = (self.i + 1) % len(self.tiles)
        waits = tuple(self.free[i].items())
        self.free[i] = {}
        return i, self.tiles[i], waits

    def release(self, i, sem, val):
        self.free[i][sem] = max(self.free[i].get(sem, 0), val)


class _Sched:
    ENGINES = ("tensor", "scalar", "vector", "gpsimd", "sync")

    def __init__(self):
        self.ops = {e: [] for e in self.ENGINES}
        self.cnt = {}

    def c(self, sem):
        return self.cnt.get(sem, 0)

    def add(self, eng, fn, waits=(), inc=None, inc_by=1):
        d = {}
        for s, v in waits:
            if v and v > d.get(s, 0):
                d[s] = v
        val = None
        if inc is not None:
            self.cnt[inc] = self.c(inc) + inc_by
            val = self.cnt[inc]
        self.ops[eng].append((fn, tuple(d.items()), inc, inc_by, val))
        return val

    def guard(self):
        return (("PE", self.c("PE")), ("ACT", self.c("ACT")),
                ("DVE", self.c("DVE")), ("ST", self.c("ST")))

    def psg(self):
        return {"ACT": self.c("ACT"), "DVE": self.c("DVE")}

    def run(self, block, sems):
        def runner(name):
            dma_q = name in ("sync", "gpsimd")
            def go(eng):
                lastw = {}
                for fn, waits, inc, inc_by, val in self.ops[name]:
                    for s, v in waits:
                        if v > lastw.get(s, 0):
                            eng.wait_ge(sems[s], v)
                            lastw[s] = v
                    if fn is None:
                        continue
                    ins = fn(eng)
                    if inc is not None:
                        ins.then_inc(sems[inc], inc_by)
                        if dma_q and inc_by == 16:
                            eng.wait_ge(sems[inc], val)
                            lastw[inc] = val
            return go
        block.tensor(runner("tensor"))
        block.scalar(runner("scalar"))
        block.vector(runner("vector"))
        block.gpsimd(runner("gpsimd"))
        block.sync(runner("sync"))


def _dma(dst, src):
    return lambda e, d=dst, s=src: e.dma_start(out=d, in_=s)


def _mm(out, lhsT, rhs, start, stop):
    return lambda e, o=out, l=lhsT, r=rhs, a=start, b=stop: e.matmul(o, l, r, start=a, stop=b)


def _act(out, in_, func, bias=0.0, scale=1.0):
    return lambda e, o=out, i=in_, f=func, b=bias, s=scale: e.activation(out=o, in_=i, func=f, bias=b, scale=s)


def _tt(out, a, b, op):
    return lambda e, o=out, x=a, y=b, p=op: e.tensor_tensor(o, x, y, p)


def _copy(out, in_):
    return lambda e, o=out, i=in_: e.tensor_copy(o, i)


def _ts1(out, in0, s1, op0):
    return lambda e, o=out, i=in0, a=s1, p=op0: e.tensor_single_scalar(o, i, a, p)


def _stt(out, in0, sc, in1, op0, op1):
    return lambda e, o=out, i=in0, s=sc, j=in1, p=op0, q=op1: e.scalar_tensor_tensor(
        out=o, in0=i, scalar=s, in1=j, op0=p, op1=q)


def _memset(ap, v):
    return lambda e, a=ap, x=v: e.memset(a, x)


class _Builder:
    def __init__(self, stage=4):
        self.stage = stage
        nc = self.nc = bass.Bass("TRN2", target_bir_lowering=False)
        self.sch = _Sched()
        ap = lambda *a, **k: nc.dram_tensor(*a, **k).ap()
        self.x_in = ap("x_in", [D, NT], F32, kind="ExternalInput")
        self.w_ffn = ap("w_ffn", [KD, 128, D], F32, kind="ExternalInput")
        self.a_cat = ap("a_cat", [128, D], F32, kind="ExternalInput")
        self.b_cat = ap("b_cat", [128, D], F32, kind="ExternalInput")
        self.bias_t = ap("bias_t", [1, D], F32, kind="ExternalInput")
        self.lw_rep = ap("lw_rep", [128, 1], F32, kind="ExternalInput")
        self.up_t = ap("up_t", [KS, 128, D], F32, kind="ExternalInput")
        self.gate_t = ap("gate_t", [KS, 128, D], F32, kind="ExternalInput")
        self.lp_t = ap("lp_t", [128, KS], F32, kind="ExternalInput")
        self.disc_t = ap("disc_t", [1, NG], F32, kind="ExternalInput")
        self.aw_t = ap("aw_t", [KD, 7, 128, SS], F32, kind="ExternalInput")
        self.wg_t = ap("wg_t", [KF, 128, D], F32, kind="ExternalInput")
        self.wu_t = ap("wu_t", [KF, 128, D], F32, kind="ExternalInput")
        self.wd_t = ap("wd_t", [KD, 128, FF], F32, kind="ExternalInput")
        self.y = ap("y", [D, NT], F32, kind="ExternalOutput")
        self.xa = ap("xa", [D, NT], F32, kind="Internal")
        self.xm = ap("xm", [D, NT], F32, kind="Internal")
        self.xbs = ap("xbs", [D, NT], F32, kind="Internal")
        self.ln2s = ap("ln2s", [D, NT], BF16, kind="Internal")
        self.h_hbm = ap("h_hbm", [FF, NT], BF16, kind="Internal")

    def emit_consts(self, es):
        nc, sch = self.nc, self.sch
        self.ones1 = es.enter_context(nc.sbuf_tensor("ones1", [1, 128], F32))
        self.onesk = es.enter_context(nc.sbuf_tensor("onesk", [128, 1], BF16))
        self.onesr = es.enter_context(nc.sbuf_tensor("onesr", [1, TS], BF16))
        self.neglp = es.enter_context(nc.sbuf_tensor("neglp", [128, KS], F32))
        self.negdc = es.enter_context(nc.sbuf_tensor("negdc", [128, 8], F32))
        self.coef = es.enter_context(nc.sbuf_tensor("coef", [128, 1], F32))
        self.epsb = es.enter_context(nc.sbuf_tensor("epsb", [1, 1], F32))
        with nc.sbuf_tensor("cscr", [128, KD], F32) as cscr, \
             nc.sbuf_tensor("drow", [1, 8], F32) as drow, \
             nc.psum_tensor("pdc", [128, 8], F32) as pdc:
            sch.add("vector", _memset(self.ones1[:], 1.0), inc="DVE")
            sch.add("vector", _memset(self.onesk[:], 1.0), inc="DVE")
            sch.add("vector", _memset(self.onesr[:], 1.0), inc="DVE")
            sch.add("vector", _memset(self.epsb[:], EPS), inc="DVE")
            v = sch.add("sync", _dma(cscr[:, 0:KS], self.lp_t[:]), inc="XL", inc_by=16)
            dv = sch.add("vector", _ts1(self.neglp[:], cscr[:, 0:KS], -1.0, ALU.mult),
                         waits=[("XL", v)], inc="DVE")
            v = sch.add("sync", _dma(cscr[:, 0:1], self.lw_rep[:]),
                        waits=[("DVE", dv)], inc="XL", inc_by=16)
            sch.add("vector", _ts1(self.coef[:], cscr[:, 0:1], 2.0, ALU.mult),
                    waits=[("XL", v)], inc="DVE")
            v = sch.add("sync", _dma(drow[:, 0:NG], self.disc_t[:]), inc="XL", inc_by=16)
            dv2 = sch.add("vector", _ts1(drow[:, 0:NG], drow[:, 0:NG], -1.0, ALU.mult),
                          waits=[("XL", v)], inc="DVE")
            pe = sch.add("tensor", _mm(pdc[:, 0:NG], self.ones1[:], drow[:, 0:NG], True, True),
                         waits=[("DVE", dv2)], inc="PE")
            sch.add("vector", _copy(self.negdc[:, 0:NG], pdc[:, 0:NG]),
                    waits=[("PE", pe)], inc="DVE")
        self.consts_dve = sch.c("DVE")

    def emit_ln(self, src, src_ready, lnt, store=None, tag=""):
        nc, sch = self.nc, self.sch
        g = sch.guard()
        st_done = {}
        with contextlib.ExitStack() as es:
            sb = lambda n, s, d: es.enter_context(nc.sbuf_tensor(n + tag, s, d))
            xs0 = sb("xs0", [128, TH], F32); xs1 = sb("xs1", [128, TH], F32)
            xb0 = sb("xb0", [128, NT], BF16); xb1 = sb("xb1", [128, NT], BF16)
            q0 = sb("q0", [128, NT], BF16); q1 = sb("q1", [128, NT], BF16)
            t1 = sb("t1", [128, TH], F32)
            rmu = sb("rmu", [1, NT], F32); re2 = sb("re2", [1, NT], F32)
            rsc = sb("rsc", [1, NT], F32); ra = sb("ra", [1, NT], F32)
            rm = sb("rm", [1, NT], F32)
            xsr = _Ring([xs0, xs1], g)
            xbr = _Ring([(xb0, q0), (xb1, q1)], g)
            with contextlib.ExitStack() as ps:
                sx = [ps.enter_context(nc.psum_tensor(f"sx{t}{tag}", [1, TS], F32))
                      for t in range(NTS)]
                sx2 = [ps.enter_context(nc.psum_tensor(f"sq{t}{tag}", [1, TS], F32))
                       for t in range(NTS)]
                psg0 = list(sch.psg().items())
                pe = 0
                for kc in range(KD):
                    j, (xbt, x2t), wj = xbr.acquire()
                    extra = [("DVE", self.consts_dve)] if kc == 0 else []
                    wl = sch.add("gpsimd", _dma(xbt[:], src[kc * 128:(kc + 1) * 128, :]),
                                 waits=list(wj) + src_ready(kc) + extra, inc="WL", inc_by=16)
                    av = sch.add("scalar", _act(x2t[:], xbt[:], AF.Square),
                                 waits=[("WL", wl)], inc="ACT")
                    for t in range(NTS):
                        w0 = [("ACT", av)] + (psg0 if (kc == 0 and t == 0) else [])
                        sch.add("tensor", _mm(sx[t][:], self.onesk[:], xbt[:, t * TS:(t + 1) * TS],
                                              kc == 0, kc == KD - 1), waits=w0)
                        pe = sch.add("tensor", _mm(sx2[t][:], self.onesk[:],
                                                   x2t[:, t * TS:(t + 1) * TS],
                                                   kc == 0, kc == KD - 1),
                                     inc="PE" if t == NTS - 1 else None) or pe
                    xbr.release(j, "PE", pe)
                for t in range(NTS):
                    sl = slice(t * TS, (t + 1) * TS)
                    sch.add("scalar", _act(rmu[:, sl], sx[t][:], AF.Copy, scale=1.0 / D),
                            waits=[("PE", pe)])
                    sch.add("scalar", _act(re2[:, sl], sx2[t][:], AF.Copy, scale=1.0 / D),
                            inc="ACT")
                a3 = sch.add("scalar", _act(rsc[:], rmu[:], AF.Square), inc="ACT")
                d1 = sch.add("vector", _tt(ra[:], re2[:], rsc[:], ALU.subtract),
                             waits=[("ACT", a3)], inc="DVE")
                a4 = sch.add("scalar", _act(rsc[:], ra[:], AF.Ln, bias=self.epsb[0:1, 0:1]),
                             waits=[("DVE", d1)], inc="ACT")
                a5 = sch.add("scalar", _act(ra[:], rsc[:], AF.Exp, scale=-0.5), inc="ACT")
                d2 = sch.add("vector", _tt(rm[:], rmu[:], ra[:], ALU.mult),
                             waits=[("ACT", a5)], inc="DVE")
            with nc.psum_tensor("pa" + tag, [128, NT], F32) as pa, \
                 nc.psum_tensor("pb" + tag, [128, NT], F32) as pb:
                pe_bc = 0
                for t in range(NTS):
                    sl = slice(t * TS, (t + 1) * TS)
                    w0 = [("ACT", a5), ("DVE", d2)] if t == 0 else []
                    sch.add("tensor", _mm(pa[:, sl], self.ones1[:], ra[:, sl], True, True),
                            waits=w0)
                    pe_bc = sch.add("tensor", _mm(pb[:, sl], self.ones1[:], rm[:, sl], True, True),
                                    inc="PE" if t == NTS - 1 else None) or pe_bc
                for kc in range(KD):
                    last_d = 0
                    for h in range(2):
                        sl = slice(h * TH, (h + 1) * TH)
                        i, xst, wi = xsr.acquire()
                        xl = sch.add("sync",
                                     _dma(xst[:], src[kc * 128:(kc + 1) * 128,
                                                      h * TH:(h + 1) * TH]),
                                     waits=list(wi) + src_ready(kc), inc="XL", inc_by=16)
                        sch.add("vector", _tt(t1[:], xst[:], pa[:, sl], ALU.mult),
                                waits=[("XL", xl), ("PE", pe_bc)], inc="DVE")
                        last_d = sch.add("vector",
                                         _tt(lnt[:, kc * NT + h * TH: kc * NT + (h + 1) * TH],
                                             t1[:], pb[:, sl], ALU.subtract), inc="DVE")
                        xsr.release(i, "DVE", last_d)
                    if store is not None:
                        sv = sch.add("sync", _dma(store[kc * 128:(kc + 1) * 128, :],
                                                  lnt[:, kc * NT:(kc + 1) * NT]),
                                     waits=[("DVE", last_d)], inc="ST", inc_by=16)
                        st_done[kc] = sv
        return sch.c("DVE"), st_done

    def emit_ffn(self, lnt, ln_done, x_src, x_ready, out_hbm,
                 tag="", w_guard=True):
        nc, sch = self.nc, self.sch
        g = sch.guard()
        done = {}
        with contextlib.ExitStack() as es:
            sb = lambda n, s, d: es.enter_context(nc.sbuf_tensor(n + tag, s, d))
            wb0 = sb("wb0", [128, D], BF16); wb1 = sb("wb1", [128, D], BF16)
            asb = sb("asb", [128, D], BF16); bsb = sb("bsb", [128, D], BF16)
            brow = sb("brow", [1, D], BF16)
            hs = sb("hs", [128, NT], BF16)
            fx0 = sb("fx0", [128, NT], F32); fx1 = sb("fx1", [128, NT], F32)
            ft0 = sb("ft0", [128, TH], F32); ft1 = sb("ft1", [128, TH], F32)
            fo0 = sb("fo0", [128, TH], F32); fo1 = sb("fo1", [128, TH], F32)
            wA = sch.add("gpsimd", _dma(asb[:], self.a_cat[:]),
                         waits=(list(g) if w_guard else []), inc="WL", inc_by=16)
            wB = sch.add("gpsimd", _dma(bsb[:], self.b_cat[:]), inc="WL", inc_by=16)
            wBr = sch.add("gpsimd", _dma(brow[:], self.bias_t[:]), inc="WL", inc_by=16)
            psg0 = list(sch.psg().items())
            with nc.psum_tensor("ph" + tag, [128, NT], F32) as ph:
                for t in range(NTS):
                    sl = slice(t * TS, (t + 1) * TS)
                    pe = 0
                    for kc in range(KD):
                        w0 = [("WL", wA), ("DVE", ln_done)] + psg0 if (t == 0 and kc == 0) else []
                        pe = sch.add("tensor",
                                     _mm(ph[:, sl], asb[:, kc * 128:(kc + 1) * 128],
                                         lnt[:, kc * NT + t * TS: kc * NT + (t + 1) * TS],
                                         kc == 0, kc == KD - 1), waits=w0,
                                     inc="PE" if kc == KD - 1 else None) or pe
                    sch.add("vector", _ts1(hs[:, sl], ph[:, sl], self.coef[:, 0:1], ALU.mult),
                            waits=[("PE", pe)], inc="DVE")
                hs_done = sch.c("DVE")
            with nc.psum_tensor("pm0" + tag, [128, NT], F32) as pm0, \
                 nc.psum_tensor("pm1" + tag, [128, NT], F32) as pm1:
                wring = _Ring([wb0, wb1], g)
                xrr = _Ring([fx0, fx1], g)
                tsr = _Ring([ft0, ft1], g)
                osr = _Ring([fo0, fo1], g)
                pm_free = [dict(sch.psg()), dict(sch.psg())]
                for o in range(KD):
                    iw, wb, ww = wring.acquire()
                    wl = sch.add("gpsimd", _dma(wb[:], self.w_ffn[o, :, :]), waits=ww,
                                 inc="WL", inc_by=16)
                    ix, xrt, wx = xrr.acquire()
                    xl = sch.add("sync", _dma(xrt[:], x_src[o * 128:(o + 1) * 128, :]),
                                 waits=list(wx) + x_ready(o), inc="XL", inc_by=16)
                    pm = (pm0, pm1)[o % 2]
                    pe = 0
                    for t in range(NTS):
                        sl = slice(t * TS, (t + 1) * TS)
                        for kc in range(KD):
                            w0 = []
                            if t == 0 and kc == 0:
                                w0 = [("WL", wl), ("DVE", hs_done)] + list(pm_free[o % 2].items())
                            sch.add("tensor",
                                    _mm(pm[:, sl], wb[:, kc * 128:(kc + 1) * 128],
                                        lnt[:, kc * NT + t * TS: kc * NT + (t + 1) * TS],
                                        kc == 0, False), waits=w0)
                        wB_w = [("WL", wB)] if (o == 0 and t == 0) else []
                        sch.add("tensor", _mm(pm[:, sl], bsb[:, o * 128:(o + 1) * 128],
                                              hs[:, sl], False, False), waits=wB_w)
                        bias_w = [("WL", wBr)] if (o == 0 and t == 0) else []
                        pe = sch.add("tensor",
                                     _mm(pm[:, sl], brow[0:1, o * 128:(o + 1) * 128],
                                         self.onesr[0:1, :], False, True), waits=bias_w,
                                     inc="PE" if t == NTS - 1 else None) or pe
                    wring.release(iw, "PE", pe)
                    dv = av = 0
                    for hh in range(2):
                        slh = slice(hh * TH, (hh + 1) * TH)
                        it, tst, wt = tsr.acquire()
                        av = sch.add("scalar", _act(tst[:], pm[:, slh], AF.Copy, scale=0.5),
                                     waits=[("PE", pe)] + list(wt), inc="ACT")
                        io, ost, wo = osr.acquire()
                        dv = sch.add("vector", _tt(ost[:], tst[:], xrt[:, slh], ALU.add),
                                     waits=[("ACT", av)] + list(wo), inc="DVE")
                        tsr.release(it, "DVE", dv)
                        sv = sch.add("sync", _dma(out_hbm[o * 128:(o + 1) * 128,
                                                          hh * TH:(hh + 1) * TH], ost[:]),
                                     waits=[("DVE", dv)], inc="ST", inc_by=16)
                        osr.release(io, "ST", sv)
                        done[(o, hh)] = sv
                    xrr.release(ix, "DVE", dv)
                    pm_free[o % 2] = {"ACT": av}
        return done

    def emit_s1(self, lnt, ln_done):
        nc, sch = self.nc, self.sch
        g = sch.guard()
        h_done = {}
        with contextlib.ExitStack() as es:
            sb = lambda n, s, d: es.enter_context(nc.sbuf_tensor(n, s, d))
            wg0 = sb("wg0", [128, D], BF16); wg1 = sb("wg1", [128, D], BF16)
            wu0 = sb("wu0", [128, D], BF16); wu1 = sb("wu1", [128, D], BF16)
            sg0 = sb("sg0", [128, TH], BF16); sg1 = sb("sg1", [128, TH], BF16)
            hst0 = sb("hst0", [128, NT], BF16); hst1 = sb("hst1", [128, NT], BF16)
            pg0 = es.enter_context(nc.psum_tensor("pg0", [128, TH], F32))
            pg1 = es.enter_context(nc.psum_tensor("pg1", [128, TH], F32))
            pu0 = es.enter_context(nc.psum_tensor("pu0", [128, TH], F32))
            pu1 = es.enter_context(nc.psum_tensor("pu1", [128, TH], F32))
            wgr = _Ring([wg0, wg1], g)
            wur = _Ring([wu0, wu1], g)
            sgr = _Ring([sg0, sg1], g)
            hr = _Ring([hst0, hst1], g)
            pg_free = [dict(sch.psg()), dict(sch.psg())]
            pu_free = [dict(sch.psg()), dict(sch.psg())]
            for f in range(KF):
                ig, wgt, wwg = wgr.acquire()
                wgl = sch.add("gpsimd", _dma(wgt[:], self.wg_t[f, :, :]), waits=wwg,
                              inc="WL", inc_by=16)
                iu, wut, wwu = wur.acquire()
                wul = sch.add("gpsimd", _dma(wut[:], self.wu_t[f, :, :]), waits=wwu,
                              inc="WL", inc_by=16)
                ih, hst, wh = hr.acquire()
                peg = peu = dv = 0
                for th in range(2):
                    pgi = (f * 2 + th) % 2
                    pg = (pg0, pg1)[pgi]
                    pu = (pu0, pu1)[pgi]
                    for ts in range(2):
                        for kc in range(KD):
                            w0 = []
                            if ts == 0 and kc == 0:
                                w0 = [("WL", wgl), ("DVE", ln_done)] + list(pg_free[pgi].items())
                            col = kc * NT + th * TH + ts * TS
                            peg = sch.add("tensor",
                                          _mm(pg[:, ts * TS:(ts + 1) * TS],
                                              wgt[:, kc * 128:(kc + 1) * 128],
                                              lnt[:, col:col + TS], kc == 0, kc == KD - 1),
                                          waits=w0,
                                          inc="PE" if (ts == 1 and kc == KD - 1) else None) or peg
                    for ts in range(2):
                        for kc in range(KD):
                            w0 = []
                            if ts == 0 and kc == 0:
                                w0 = [("WL", wul)] + list(pu_free[pgi].items())
                            col = kc * NT + th * TH + ts * TS
                            peu = sch.add("tensor",
                                          _mm(pu[:, ts * TS:(ts + 1) * TS],
                                              wut[:, kc * 128:(kc + 1) * 128],
                                              lnt[:, col:col + TS], kc == 0, kc == KD - 1),
                                          waits=w0,
                                          inc="PE" if (ts == 1 and kc == KD - 1) else None) or peu
                    isg, sgt, wsg = sgr.acquire()
                    av = sch.add("scalar", _act(sgt[:], pg[:], AF.Silu),
                                 waits=[("PE", peg)] + list(wsg), inc="ACT")
                    pg_free[pgi] = {"ACT": av}
                    dv = sch.add("vector", _tt(hst[:, th * TH:(th + 1) * TH], sgt[:], pu[:],
                                               ALU.mult),
                                 waits=[("ACT", av), ("PE", peu)] + (list(wh) if th == 0 else []),
                                 inc="DVE")
                    pu_free[pgi] = {"DVE": dv}
                    sgr.release(isg, "DVE", dv)
                wgr.release(ig, "PE", peg)
                wur.release(iu, "PE", peu)
                sv = sch.add("sync", _dma(self.h_hbm[f * 128:(f + 1) * 128, :], hst[:]),
                             waits=[("DVE", dv)], inc="ST", inc_by=16)
                hr.release(ih, "ST", sv)
                h_done[f] = sv
        return h_done

    def emit_s2(self, xa_done, h_done, dst):
        nc, sch = self.nc, self.sch
        g = sch.guard()
        xm_done = {}
        with contextlib.ExitStack() as es:
            sb = lambda n, s, d: es.enter_context(nc.sbuf_tensor(n, s, d))
            hbuf = sb("hbuf", [128, KF * TH], BF16)
            wd0 = sb("wd0", [128, FF // 2], BF16); wd1 = sb("wd1", [128, FF // 2], BF16)
            sx0b = sb("sx0b", [128, TH], F32); sx1b = sb("sx1b", [128, TH], F32)
            so0 = sb("so0", [128, TH], F32); so1 = sb("so1", [128, TH], F32)
            spm0 = es.enter_context(nc.psum_tensor("spm0", [128, TH], F32))
            spm1 = es.enter_context(nc.psum_tensor("spm1", [128, TH], F32))
            xrr = _Ring([sx0b, sx1b], g)
            osr = _Ring([so0, so1], g)
            pm_free = [dict(sch.psg()), dict(sch.psg())]
            wd_free = [dict(g), dict(g)]
            last_pe = 0
            for th in range(2):
                hl = []
                for grp in range(6):
                    f0 = grp * 13
                    w0 = [("ST", h_done[f0 + 12])]
                    w0 += list(g) if th == 0 else [("PE", last_pe)]
                    hv = sch.add(
                        "sync",
                        _dma(hbuf[:, f0 * TH:(f0 + 13) * TH].rearrange("p (f t) -> p f t", t=TH),
                             self.h_hbm[f0 * 128:(f0 + 13) * 128,
                                        th * TH:(th + 1) * TH].rearrange("(f p) t -> p f t", p=128)),
                        waits=w0, inc="XL", inc_by=16)
                    hl.append(hv)
                for o in range(KD):
                    w0l = sch.add("gpsimd", _dma(wd0[:], self.wd_t[o, :, 0:FF // 2]),
                                  waits=list(wd_free[0].items()), inc="WL", inc_by=16)
                    w1l = sch.add("gpsimd", _dma(wd1[:], self.wd_t[o, :, FF // 2:FF]),
                                  waits=list(wd_free[1].items()), inc="WL", inc_by=16)
                    ix, xrt, wx = xrr.acquire()
                    xl = sch.add("sync", _dma(xrt[:], self.xa[o * 128:(o + 1) * 128,
                                                             th * TH:(th + 1) * TH]),
                                 waits=list(wx) + [("ST", xa_done[(o, th)])],
                                 inc="XL", inc_by=16)
                    pmi = o % 2
                    pm = (spm0, spm1)[pmi]
                    pe = 0
                    for ts in range(2):
                        for kc in range(KF):
                            w0 = []
                            if ts == 0 and kc == 0:
                                w0 = [("WL", w0l)] + list(pm_free[pmi].items())
                            if kc == 39 and ts == 0:
                                w0 = [("WL", w1l)]
                            if o == 0 and ts == 0 and kc % 13 == 0:
                                w0 = w0 + [("XL", hl[kc // 13])]
                            wdt = (wd0, wd1)[kc // 39]
                            jj = kc % 39
                            pe = sch.add("tensor",
                                         _mm(pm[:, ts * TS:(ts + 1) * TS],
                                             wdt[:, jj * 128:(jj + 1) * 128],
                                             hbuf[:, kc * TH + ts * TS: kc * TH + (ts + 1) * TS],
                                             kc == 0, kc == KF - 1), waits=w0,
                                         inc="PE" if (ts == 1 and kc == KF - 1) else None) or pe
                    wd_free[0] = {"PE": pe}
                    wd_free[1] = {"PE": pe}
                    io, ost, wo = osr.acquire()
                    dv = sch.add("vector", _stt(ost[:], pm[:], 1.0, xrt[:], ALU.mult, ALU.add),
                                 waits=[("PE", pe), ("XL", xl)] + list(wo), inc="DVE")
                    pm_free[pmi] = {"DVE": dv}
                    xrr.release(ix, "DVE", dv)
                    sv = sch.add("sync", _dma(dst[o * 128:(o + 1) * 128,
                                                  th * TH:(th + 1) * TH], ost[:]),
                                 waits=[("DVE", dv)], inc="ST", inc_by=16)
                    osr.release(io, "ST", sv)
                    xm_done[(o, th)] = sv
                last_pe = sch.c("PE")
        return xm_done

    def emit_approx_block(self, b, ln2_st, xm_done, dst, xb_done):
        nc, sch = self.nc, self.sch
        tagb = f"X{b}"
        with contextlib.ExitStack() as eo:
            au = eo.enter_context(nc.sbuf_tensor("au" + tagb, [128, KS * TH], BF16))
            dl = eo.enter_context(nc.sbuf_tensor("dl" + tagb, [128, KS * TH], F32))
            self._approx_updelta(b, tagb, au, dl, ln2_st)
            with contextlib.ExitStack() as et:
                tm = [eo if False else et.enter_context(
                    nc.sbuf_tensor(f"tm{m}{tagb}", [128, KS * TH], BF16)) for m in range(1, 7)]
                auk = et.enter_context(nc.sbuf_tensor("auk" + tagb, [128, KS * TH], BF16))
                tmps_done = self._approx_tmps(b, tagb, au, dl, tm, auk)
                self._approx_mms(b, tagb, [au] + tm, tmps_done, xm_done, dst, xb_done)

    def _approx_updelta(self, b, tagb, au, dl, ln2_st):
        nc, sch = self.nc, self.sch
        g = sch.guard()
        with contextlib.ExitStack() as es:
            sb = lambda n, s, d: es.enter_context(nc.sbuf_tensor(n + tagb, s, d))
            lnb = sb("lnb", [128, KD * TH], BF16)
            ub0 = sb("ub0", [128, D], BF16); ub1 = sb("ub1", [128, D], BF16)
            gb0 = sb("gb0", [128, D], BF16); gb1 = sb("gb1", [128, D], BF16)
            pau0 = es.enter_context(nc.psum_tensor("pau0" + tagb, [128, TH], F32))
            pau1 = es.enter_context(nc.psum_tensor("pau1" + tagb, [128, TH], F32))
            pdl0 = es.enter_context(nc.psum_tensor("pdl0" + tagb, [128, TH], F32))
            pdl1 = es.enter_context(nc.psum_tensor("pdl1" + tagb, [128, TH], F32))
            lnl = 0
            for grp in range(4):
                k0 = grp * 8
                lnl = sch.add(
                    "sync",
                    _dma(lnb[:, k0 * TH:(k0 + 8) * TH].rearrange("p (k t) -> p k t", t=TH),
                         self.ln2s[k0 * 128:(k0 + 8) * 128,
                                   b * TH:(b + 1) * TH].rearrange("(k p) t -> p k t", p=128)),
                    waits=list(g) + [("ST", ln2_st[k0 + 7])], inc="XL", inc_by=16)
            ur = _Ring([ub0, ub1], g)
            gr = _Ring([gb0, gb1], g)
            pau_free = [dict(sch.psg()), dict(sch.psg())]
            pdl_free = [dict(sch.psg()), dict(sch.psg())]
            for os_ in range(KS):
                iu, ubt, wwu = ur.acquire()
                ul = sch.add("gpsimd", _dma(ubt[:], self.up_t[os_, :, :]), waits=wwu,
                             inc="WL", inc_by=16)
                ig, gbt, wwg = gr.acquire()
                gl = sch.add("gpsimd", _dma(gbt[:], self.gate_t[os_, :, :]), waits=wwg,
                             inc="WL", inc_by=16)
                pi = os_ % 2
                pau = (pau0, pau1)[pi]
                pdl = (pdl0, pdl1)[pi]
                peu = peg = 0
                for ts in range(2):
                    for kc in range(KD):
                        w0 = []
                        if ts == 0 and kc == 0:
                            w0 = [("WL", ul), ("XL", lnl)] + list(pau_free[pi].items())
                        peu = sch.add("tensor",
                                      _mm(pau[:, ts * TS:(ts + 1) * TS],
                                          ubt[:, kc * 128:(kc + 1) * 128],
                                          lnb[:, kc * TH + ts * TS: kc * TH + (ts + 1) * TS],
                                          kc == 0, kc == KD - 1), waits=w0,
                                      inc="PE" if (ts == 1 and kc == KD - 1) else None) or peu
                av_u = sch.add("scalar", _act(au[:, os_ * TH:(os_ + 1) * TH], pau[:], AF.Copy),
                               waits=[("PE", peu)], inc="ACT")
                pau_free[pi] = {"ACT": av_u}
                for ts in range(2):
                    for kc in range(KD):
                        w0 = []
                        if ts == 0 and kc == 0:
                            w0 = [("WL", gl)] + list(pdl_free[pi].items())
                        peg = sch.add("tensor",
                                      _mm(pdl[:, ts * TS:(ts + 1) * TS],
                                          gbt[:, kc * 128:(kc + 1) * 128],
                                          lnb[:, kc * TH + ts * TS: kc * TH + (ts + 1) * TS],
                                          kc == 0, kc == KD - 1), waits=w0,
                                      inc="PE" if (ts == 1 and kc == KD - 1) else None) or peg
                dv_d = sch.add("vector", _ts1(dl[:, os_ * TH:(os_ + 1) * TH], pdl[:],
                                              self.neglp[:, os_:os_ + 1], ALU.add),
                               waits=[("PE", peg)], inc="DVE")
                pdl_free[pi] = {"DVE": dv_d}
                ur.release(iu, "PE", peu)
                gr.release(ig, "PE", peg)

    def _approx_tmps(self, b, tagb, au, dl, tm, auk):
        nc, sch = self.nc, self.sch
        with contextlib.ExitStack() as es:
            sb = lambda n: es.enter_context(nc.sbuf_tensor(n + tagb, [128, TH], F32))
            absd = sb("absd"); logd = sb("logd"); keep = sb("keep")
            dta = sb("dta"); dtb = sb("dtb"); uu = sb("uu")
            dts = (dta, dtb)
            dt_free = {0: 0, 1: 0}
            for kc in range(KS):
                sl = slice(kc * TH, (kc + 1) * TH)
                sch.add("scalar", _act(absd[:], dl[:, sl], AF.Abs), inc="ACT")
                a_log = sch.add("scalar", _act(logd[:], absd[:], AF.Ln), inc="ACT")
                sch.add("vector", _ts1(keep[:], absd[:], THD, ALU.is_le),
                        waits=[("ACT", a_log)], inc="DVE")
                sch.add("vector", _tt(auk[:, sl], au[:, sl], keep[:], ALU.mult), inc="DVE")
                sch.add("vector", _tt(tm[0][:, sl], dl[:, sl], au[:, sl], ALU.mult), inc="DVE")
                for m in range(2, 7):
                    dt = dts[m % 2]
                    a_dt = sch.add("scalar",
                                   _act(dt[:], logd[:], AF.Exp, bias=self.negdc[:, m - 1:m],
                                        scale=float(m if m % 2 == 0 else m - 1)),
                                   waits=[("DVE", dt_free[m % 2])], inc="ACT")
                    src2 = auk if m > 4 else au
                    if m % 2 == 0:
                        dvv = sch.add("vector", _tt(tm[m - 1][:, sl], dt[:], src2[:, sl], ALU.mult),
                                      waits=[("ACT", a_dt)], inc="DVE")
                    else:
                        sch.add("vector", _tt(uu[:], dt[:], dl[:, sl], ALU.mult),
                                waits=[("ACT", a_dt)], inc="DVE")
                        dvv = sch.add("vector", _tt(tm[m - 1][:, sl], uu[:], src2[:, sl], ALU.mult),
                                      inc="DVE")
                    dt_free[m % 2] = dvv
            return sch.c("DVE")

    def _approx_mms(self, b, tagb, tmps, tmps_done, xm_done, dst, xb_done):
        nc, sch = self.nc, self.sch
        g2 = sch.guard()
        with contextlib.ExitStack() as es:
            sb = lambda n, s, d: es.enter_context(nc.sbuf_tensor(n + tagb, s, d))
            awb0 = sb("awb0", [128, 7 * SS], BF16); awb1 = sb("awb1", [128, 7 * SS], BF16)
            xmr0 = sb("xmr0", [128, TH], F32); xmr1 = sb("xmr1", [128, TH], F32)
            ao0 = sb("ao0", [128, TH], F32); ao1 = sb("ao1", [128, TH], F32)
            apm0 = es.enter_context(nc.psum_tensor("apm0" + tagb, [128, TH], F32))
            apm1 = es.enter_context(nc.psum_tensor("apm1" + tagb, [128, TH], F32))
            awr = _Ring([awb0, awb1], g2)
            xrr = _Ring([xmr0, xmr1], g2)
            osr = _Ring([ao0, ao1], g2)
            pm_free = [dict(sch.psg()), dict(sch.psg())]
            for o in range(KD):
                ia, awt, wa = awr.acquire()
                awl = sch.add("gpsimd",
                              _dma(awt[:].rearrange("p (m k) -> p m k", k=SS),
                                   self.aw_t[o, :, :, :].rearrange("m p k -> p m k")),
                              waits=wa, inc="WL", inc_by=16)
                ix, xrt, wx = xrr.acquire()
                xl = sch.add("sync", _dma(xrt[:], self.xm[o * 128:(o + 1) * 128,
                                                          b * TH:(b + 1) * TH]),
                             waits=list(wx) + [("ST", xm_done[(o, b)])], inc="XL", inc_by=16)
                pmi = o % 2
                pm = (apm0, apm1)[pmi]
                pe = 0
                for ts in range(2):
                    for m in range(7):
                        for kc in range(KS):
                            w0 = []
                            if ts == 0 and m == 0 and kc == 0:
                                w0 = [("WL", awl), ("DVE", tmps_done)] + list(pm_free[pmi].items())
                            pe = sch.add(
                                "tensor",
                                _mm(pm[:, ts * TS:(ts + 1) * TS],
                                    awt[:, m * SS + kc * 128: m * SS + (kc + 1) * 128],
                                    tmps[m][:, kc * TH + ts * TS: kc * TH + (ts + 1) * TS],
                                    (m == 0 and kc == 0), (m == 6 and kc == KS - 1)),
                                waits=w0,
                                inc="PE" if (ts == 1 and m == 6 and kc == KS - 1) else None) or pe
                awr.release(ia, "PE", pe)
                io, ost, wo = osr.acquire()
                dv = sch.add("vector", _stt(ost[:], pm[:], 1.0, xrt[:], ALU.mult, ALU.add),
                             waits=[("PE", pe), ("XL", xl)] + list(wo), inc="DVE")
                pm_free[pmi] = {"DVE": dv}
                xrr.release(ix, "DVE", dv)
                sv = sch.add("sync", _dma(dst[o * 128:(o + 1) * 128, b * TH:(b + 1) * TH],
                                          ost[:]),
                             waits=[("DVE", dv)], inc="ST", inc_by=16)
                osr.release(io, "ST", sv)
                xb_done[(o, b)] = sv

    def build(self):
        nc, sch, stage = self.nc, self.sch, self.stage
        ffn1_dst = self.xa if stage > 1 else self.y
        s2_dst = self.xm if stage > 2 else self.y
        ap_dst = self.xbs if stage > 3 else self.y
        with contextlib.ExitStack() as es:
            sems = {n: es.enter_context(nc.semaphore(n))
                    for n in ("WL", "XL", "ST", "PE", "ACT", "DVE")}
            self.emit_consts(es)

            with contextlib.ExitStack() as e1:
                lnA = e1.enter_context(nc.sbuf_tensor("lnA", [128, KD * NT], BF16))
                ln1_done, _ = self.emit_ln(self.x_in, lambda kc: [], lnA, tag="L1")
                if stage == 0:
                    for kc in range(KD):
                        sch.add("gpsimd", _dma(self.y[kc * 128:(kc + 1) * 128, :],
                                               lnA[:, kc * NT:(kc + 1) * NT]),
                                waits=[("DVE", ln1_done)], inc="ST", inc_by=16)
                    xa_done = {}
                else:
                    xa_done = self.emit_ffn(lnA, ln1_done, self.x_in, lambda o: [],
                                            ffn1_dst, tag="F1")

            if stage >= 2:
                with contextlib.ExitStack() as e2:
                    lnB = e2.enter_context(nc.sbuf_tensor("lnB", [128, KD * NT], BF16))
                    ln2_done, ln2_st = self.emit_ln(
                        self.xa, lambda kc: [("ST", xa_done[(kc, 1)])], lnB,
                        store=self.ln2s, tag="L2")
                    h_done = self.emit_s1(lnB, ln2_done)
                xm_done = self.emit_s2(xa_done, h_done, s2_dst)

            if stage >= 3:
                xb_done = {}
                for b in range(2):
                    self.emit_approx_block(b, ln2_st, xm_done, ap_dst, xb_done)

            if stage >= 4:
                with contextlib.ExitStack() as e4:
                    lnC = e4.enter_context(nc.sbuf_tensor("lnC", [128, KD * NT], BF16))
                    ln3_done, _ = self.emit_ln(self.xbs,
                                               lambda kc: [("ST", xb_done[(kc, 1)])],
                                               lnC, tag="L3")
                    self.emit_ffn(lnC, ln3_done, self.xbs,
                                  lambda o: [("ST", xb_done[(o, 1)])], self.y,
                                  tag="F2")

            sch.add("sync", None, waits=[("ST", sch.c("ST"))])
            with nc.Block() as block:
                sch.run(block, sems)
        return nc


def build_nc(stage=4):
    return _Builder(stage).build()


def _tile_lhsT(W):
    W = np.asarray(W, np.float32)
    dout, din = W.shape
    nk, no = din // 128, dout // 128
    return np.ascontiguousarray(
        W.T.reshape(nk, 128, no, 128).transpose(2, 1, 0, 3).reshape(no, 128, nk * 128))


def prep_inputs(inp):
    d = {k: np.asarray(v, np.float32) for k, v in inp.items()}
    x = np.ascontiguousarray(d["L0_attn_hidden_states"].reshape(NCORES * NT, D))
    shards = [np.ascontiguousarray(x[c * NT:(c + 1) * NT].T) for c in range(NCORES)]
    W = {}
    W["w_ffn"] = _tile_lhsT(d["L0_ffn_W"])
    W["a_cat"] = _tile_lhsT(d["L0_ffn_lora_A"].reshape(128, D))[0]
    W["b_cat"] = np.ascontiguousarray(d["L0_ffn_lora_B"].transpose(0, 2, 1).reshape(128, D))
    W["bias_t"] = np.ascontiguousarray(d["L0_ffn_bias"].reshape(1, D))
    W["lw_rep"] = np.ascontiguousarray(np.repeat(d["L0_ffn_lora_weights"], 16).reshape(128, 1))
    W["up_t"] = _tile_lhsT(d["L0_attn_up_w"])
    W["gate_t"] = _tile_lhsT(d["L0_attn_gate_w"])
    W["lp_t"] = np.ascontiguousarray(d["L0_attn_local_point"].reshape(KS, 128).T)
    W["disc_t"] = np.ascontiguousarray(d["L0_attn_discount_factor"].reshape(1, NG))
    la = _tile_lhsT(d["L0_attn_local_approx_output"])
    fu = [_tile_lhsT(np.ascontiguousarray(d["L0_attn_fuse_weight"][:, :, i])) for i in range(NG)]
    W["aw_t"] = np.ascontiguousarray(np.stack([la] + fu, axis=1))
    W["wg_t"] = _tile_lhsT(d["L0_attn_ffn_gate_w"])
    W["wu_t"] = _tile_lhsT(d["L0_attn_ffn_up_w"])
    W["wd_t"] = _tile_lhsT(d["L0_attn_ffn_down_w"])
    return W, shards


def kernel(**inp):
    W, shards = prep_inputs(inp)
    nc = build_nc(4)
    in_maps = [dict(W, x_in=shards[c]) for c in range(NCORES)]
    res = run_bass_kernel_spmd(nc, in_maps, core_ids=list(range(NCORES)))
    out = np.stack([np.asarray(res.results[c]["y"]) for c in range(NCORES)])
    out = out.transpose(0, 2, 1).reshape(2, NCORES * NT // 2, D)
    return np.ascontiguousarray(out)


# version 03
# speedup vs baseline: 1.1741x; 1.1741x over previous
import contextlib
import numpy as np
import concourse.bass as bass
import concourse.mybir as mybir
from concourse.bass_utils import run_bass_kernel_spmd

F32 = mybir.dt.float32
BF16 = mybir.dt.bfloat16
AF = mybir.ActivationFunctionType
ALU = mybir.AluOpType

NCORES = 8
D = 4096
SS = 1024
FF = 9984
NG = 6
KD = D // 128
KS = SS // 128
KF = FF // 128
NT = 2048
TS = 512
NTS = NT // TS
TH = NT // 2
THD = 2.5
EPS = 1e-5


class _Ring:
    def __init__(self, tiles, guard=()):
        self.tiles = list(tiles)
        self.free = [dict(guard) for _ in self.tiles]
        self.i = 0

    def acquire(self):
        i = self.i
        self.i = (self.i + 1) % len(self.tiles)
        waits = tuple(self.free[i].items())
        self.free[i] = {}
        return i, self.tiles[i], waits

    def release(self, i, sem, val):
        self.free[i][sem] = max(self.free[i].get(sem, 0), val)


class _Sched:
    ENGINES = ("tensor", "scalar", "vector", "gpsimd", "sync")

    def __init__(self):
        self.ops = {e: [] for e in self.ENGINES}
        self.cnt = {}

    def c(self, sem):
        return self.cnt.get(sem, 0)

    def add(self, eng, fn, waits=(), inc=None, inc_by=1):
        d = {}
        for s, v in waits:
            if v and v > d.get(s, 0):
                d[s] = v
        val = None
        if inc is not None:
            self.cnt[inc] = self.c(inc) + inc_by
            val = self.cnt[inc]
        self.ops[eng].append((fn, tuple(d.items()), inc, inc_by, val))
        return val

    def guard(self):
        return (("PE", self.c("PE")), ("ACT", self.c("ACT")),
                ("DVE", self.c("DVE")), ("ST", self.c("ST")))

    def psg(self):
        return {"ACT": self.c("ACT"), "DVE": self.c("DVE")}

    def run(self, block, sems):
        def runner(name):
            dma_q = name in ("sync", "gpsimd")
            def go(eng):
                lastw = {}
                for fn, waits, inc, inc_by, val in self.ops[name]:
                    for s, v in waits:
                        if v > lastw.get(s, 0):
                            eng.wait_ge(sems[s], v)
                            lastw[s] = v
                    if fn is None:
                        continue
                    ins = fn(eng)
                    if inc is not None:
                        ins.then_inc(sems[inc], inc_by)
                        if dma_q and inc_by == 16:
                            eng.wait_ge(sems[inc], val)
                            lastw[inc] = val
            return go
        block.tensor(runner("tensor"))
        block.scalar(runner("scalar"))
        block.vector(runner("vector"))
        block.gpsimd(runner("gpsimd"))
        block.sync(runner("sync"))


def _dma(dst, src):
    return lambda e, d=dst, s=src: e.dma_start(out=d, in_=s)


def _mm(out, lhsT, rhs, start, stop):
    return lambda e, o=out, l=lhsT, r=rhs, a=start, b=stop: e.matmul(o, l, r, start=a, stop=b)


def _act(out, in_, func, bias=0.0, scale=1.0):
    return lambda e, o=out, i=in_, f=func, b=bias, s=scale: e.activation(out=o, in_=i, func=f, bias=b, scale=s)


def _tt(out, a, b, op):
    return lambda e, o=out, x=a, y=b, p=op: e.tensor_tensor(o, x, y, p)


def _copy(out, in_):
    return lambda e, o=out, i=in_: e.tensor_copy(o, i)


def _ts1(out, in0, s1, op0):
    return lambda e, o=out, i=in0, a=s1, p=op0: e.tensor_single_scalar(o, i, a, p)


def _stt(out, in0, sc, in1, op0, op1):
    return lambda e, o=out, i=in0, s=sc, j=in1, p=op0, q=op1: e.scalar_tensor_tensor(
        out=o, in0=i, scalar=s, in1=j, op0=p, op1=q)


def _memset(ap, v):
    return lambda e, a=ap, x=v: e.memset(a, x)


class _Builder:
    def __init__(self, stage=4):
        self.stage = stage
        nc = self.nc = bass.Bass("TRN2", target_bir_lowering=False)
        self.sch = _Sched()
        ap = lambda *a, **k: nc.dram_tensor(*a, **k).ap()
        self.x_in = ap("x_in", [D, NT], F32, kind="ExternalInput")
        self.w_ffn = ap("w_ffn", [KD, 128, D], F32, kind="ExternalInput")
        self.a_cat = ap("a_cat", [128, D], F32, kind="ExternalInput")
        self.b_cat = ap("b_cat", [128, D], F32, kind="ExternalInput")
        self.bias_t = ap("bias_t", [128, KD], F32, kind="ExternalInput")
        self.lw_rep = ap("lw_rep", [128, 1], F32, kind="ExternalInput")
        self.up_t = ap("up_t", [KS, 128, D], F32, kind="ExternalInput")
        self.gate_t = ap("gate_t", [KS, 128, D], F32, kind="ExternalInput")
        self.lp_t = ap("lp_t", [128, KS], F32, kind="ExternalInput")
        self.disc_t = ap("disc_t", [1, NG], F32, kind="ExternalInput")
        self.aw_t = ap("aw_t", [KD, 7, 128, SS], F32, kind="ExternalInput")
        self.wg_t = ap("wg_t", [KF, 128, D], F32, kind="ExternalInput")
        self.wu_t = ap("wu_t", [KF, 128, D], F32, kind="ExternalInput")
        self.wd_t = ap("wd_t", [KD, 128, FF], F32, kind="ExternalInput")
        self.y = ap("y", [D, NT], F32, kind="ExternalOutput")
        self.xa = ap("xa", [D, NT], F32, kind="Internal")
        self.xm = ap("xm", [D, NT], F32, kind="Internal")
        self.xbs = ap("xbs", [D, NT], F32, kind="Internal")
        self.ln2s = ap("ln2s", [D, NT], BF16, kind="Internal")
        self.h_hbm = ap("h_hbm", [FF, NT], BF16, kind="Internal")

    def emit_consts(self, es):
        nc, sch = self.nc, self.sch
        self.ones1 = es.enter_context(nc.sbuf_tensor("ones1", [1, 128], F32))
        self.onesk = es.enter_context(nc.sbuf_tensor("onesk", [128, 1], BF16))
        self.bhalf = es.enter_context(nc.sbuf_tensor("bhalf", [128, KD], F32))
        self.neglp = es.enter_context(nc.sbuf_tensor("neglp", [128, KS], F32))
        self.negdc = es.enter_context(nc.sbuf_tensor("negdc", [128, 8], F32))
        self.coef = es.enter_context(nc.sbuf_tensor("coef", [128, 1], F32))
        self.epsb = es.enter_context(nc.sbuf_tensor("epsb", [1, 1], F32))
        with nc.sbuf_tensor("cscr", [128, KD], F32) as cscr, \
             nc.sbuf_tensor("drow", [1, 8], F32) as drow, \
             nc.psum_tensor("pdc", [128, 8], F32) as pdc:
            sch.add("vector", _memset(self.ones1[:], 1.0), inc="DVE")
            sch.add("vector", _memset(self.onesk[:], 1.0), inc="DVE")
            sch.add("vector", _memset(self.epsb[:], EPS), inc="DVE")
            v = sch.add("sync", _dma(cscr[:], self.bias_t[:]), inc="XL", inc_by=16)
            dv = sch.add("vector", _ts1(self.bhalf[:], cscr[:], 0.5, ALU.mult),
                         waits=[("XL", v)], inc="DVE")
            v = sch.add("sync", _dma(cscr[:, 0:KS], self.lp_t[:]),
                        waits=[("DVE", dv)], inc="XL", inc_by=16)
            dv = sch.add("vector", _ts1(self.neglp[:], cscr[:, 0:KS], -1.0, ALU.mult),
                         waits=[("XL", v)], inc="DVE")
            v = sch.add("sync", _dma(cscr[:, 0:1], self.lw_rep[:]),
                        waits=[("DVE", dv)], inc="XL", inc_by=16)
            sch.add("vector", _ts1(self.coef[:], cscr[:, 0:1], 2.0, ALU.mult),
                    waits=[("XL", v)], inc="DVE")
            v = sch.add("sync", _dma(drow[:, 0:NG], self.disc_t[:]), inc="XL", inc_by=16)
            dv2 = sch.add("vector", _ts1(drow[:, 0:NG], drow[:, 0:NG], -1.0, ALU.mult),
                          waits=[("XL", v)], inc="DVE")
            pe = sch.add("tensor", _mm(pdc[:, 0:NG], self.ones1[:], drow[:, 0:NG], True, True),
                         waits=[("DVE", dv2)], inc="PE")
            sch.add("vector", _copy(self.negdc[:, 0:NG], pdc[:, 0:NG]),
                    waits=[("PE", pe)], inc="DVE")
        self.consts_dve = sch.c("DVE")

    def emit_ln(self, src, src_ready, lnt, store=None, tag=""):
        nc, sch = self.nc, self.sch
        g = sch.guard()
        st_done = {}
        with contextlib.ExitStack() as es:
            sb = lambda n, s, d: es.enter_context(nc.sbuf_tensor(n + tag, s, d))
            q0 = sb("q0", [128, NT], BF16); q1 = sb("q1", [128, NT], BF16)
            t1 = sb("t1", [128, TH], F32)
            rmu = sb("rmu", [1, NT], F32); re2 = sb("re2", [1, NT], F32)
            rsc = sb("rsc", [1, NT], F32); ra = sb("ra", [1, NT], F32)
            rm = sb("rm", [1, NT], F32)
            qr = _Ring([q0, q1], g)
            with contextlib.ExitStack() as ps:
                sx = [ps.enter_context(nc.psum_tensor(f"sx{t}{tag}", [1, TS], F32))
                      for t in range(NTS)]
                sx2 = [ps.enter_context(nc.psum_tensor(f"sq{t}{tag}", [1, TS], F32))
                       for t in range(NTS)]
                psg0 = list(sch.psg().items())
                pe = 0
                for kc in range(KD):
                    xbt = lnt[:, kc * NT:(kc + 1) * NT]
                    j, x2t, wj = qr.acquire()
                    extra = list(g) + [("DVE", self.consts_dve)] if kc == 0 else []
                    wl = sch.add("gpsimd", _dma(xbt, src[kc * 128:(kc + 1) * 128, :]),
                                 waits=list(wj) + src_ready(kc) + extra, inc="WL", inc_by=16)
                    av = sch.add("scalar", _act(x2t[:], xbt, AF.Square),
                                 waits=[("WL", wl)], inc="ACT")
                    for t in range(NTS):
                        w0 = [("ACT", av)] + (psg0 if (kc == 0 and t == 0) else [])
                        sch.add("tensor", _mm(sx[t][:], self.onesk[:],
                                              lnt[:, kc * NT + t * TS: kc * NT + (t + 1) * TS],
                                              kc == 0, kc == KD - 1), waits=w0)
                        pe = sch.add("tensor", _mm(sx2[t][:], self.onesk[:],
                                                   x2t[:, t * TS:(t + 1) * TS],
                                                   kc == 0, kc == KD - 1),
                                     inc="PE" if t == NTS - 1 else None) or pe
                    qr.release(j, "PE", pe)
                for t in range(NTS):
                    sl = slice(t * TS, (t + 1) * TS)
                    sch.add("scalar", _act(rmu[:, sl], sx[t][:], AF.Copy, scale=1.0 / D),
                            waits=[("PE", pe)])
                    sch.add("scalar", _act(re2[:, sl], sx2[t][:], AF.Copy, scale=1.0 / D),
                            inc="ACT")
                a3 = sch.add("scalar", _act(rsc[:], rmu[:], AF.Square), inc="ACT")
                d1 = sch.add("vector", _tt(ra[:], re2[:], rsc[:], ALU.subtract),
                             waits=[("ACT", a3)], inc="DVE")
                a4 = sch.add("scalar", _act(rsc[:], ra[:], AF.Ln, bias=self.epsb[0:1, 0:1]),
                             waits=[("DVE", d1)], inc="ACT")
                a5 = sch.add("scalar", _act(ra[:], rsc[:], AF.Exp, scale=-0.5), inc="ACT")
                d2 = sch.add("vector", _tt(rm[:], rmu[:], ra[:], ALU.mult),
                             waits=[("ACT", a5)], inc="DVE")
            with nc.psum_tensor("pa" + tag, [128, NT], F32) as pa, \
                 nc.psum_tensor("pb" + tag, [128, NT], F32) as pb:
                pe_bc = 0
                for t in range(NTS):
                    sl = slice(t * TS, (t + 1) * TS)
                    w0 = [("ACT", a5), ("DVE", d2)] if t == 0 else []
                    sch.add("tensor", _mm(pa[:, sl], self.ones1[:], ra[:, sl], True, True),
                            waits=w0)
                    pe_bc = sch.add("tensor", _mm(pb[:, sl], self.ones1[:], rm[:, sl], True, True),
                                    inc="PE" if t == NTS - 1 else None) or pe_bc
                for kc in range(KD):
                    last_d = 0
                    for h in range(2):
                        sl = slice(h * TH, (h + 1) * TH)
                        cs = slice(kc * NT + h * TH, kc * NT + (h + 1) * TH)
                        sch.add("vector", _tt(t1[:], lnt[:, cs], pa[:, sl], ALU.mult),
                                waits=[("PE", pe_bc)], inc="DVE")
                        last_d = sch.add("vector", _tt(lnt[:, cs], t1[:], pb[:, sl],
                                                       ALU.subtract), inc="DVE")
                    if store is not None:
                        sv = sch.add("sync", _dma(store[kc * 128:(kc + 1) * 128, :],
                                                  lnt[:, kc * NT:(kc + 1) * NT]),
                                     waits=[("DVE", last_d)], inc="ST", inc_by=16)
                        st_done[kc] = sv
        return sch.c("DVE"), st_done

    def emit_ffn(self, lnt, ln_done, x_src, x_ready, out_hbm,
                 tag="", w_guard=True):
        nc, sch = self.nc, self.sch
        g = sch.guard()
        done = {}
        with contextlib.ExitStack() as es:
            sb = lambda n, s, d: es.enter_context(nc.sbuf_tensor(n + tag, s, d))
            wb0 = sb("wb0", [128, D], BF16); wb1 = sb("wb1", [128, D], BF16)
            asb = sb("asb", [128, D], BF16); bsb = sb("bsb", [128, D], BF16)
            hs = sb("hs", [128, NT], BF16)
            fx0 = sb("fx0", [128, NT], F32); fx1 = sb("fx1", [128, NT], F32)
            ft0 = sb("ft0", [128, TH], F32); ft1 = sb("ft1", [128, TH], F32)
            fo0 = sb("fo0", [128, TH], F32); fo1 = sb("fo1", [128, TH], F32)
            wA = sch.add("gpsimd", _dma(asb[:], self.a_cat[:]),
                         waits=(list(g) if w_guard else []), inc="WL", inc_by=16)
            wB = sch.add("gpsimd", _dma(bsb[:], self.b_cat[:]), inc="WL", inc_by=16)
            psg0 = list(sch.psg().items())
            with nc.psum_tensor("ph" + tag, [128, NT], F32) as ph:
                for t in range(NTS):
                    sl = slice(t * TS, (t + 1) * TS)
                    pe = 0
                    for kc in range(KD):
                        w0 = [("WL", wA), ("DVE", ln_done)] + psg0 if (t == 0 and kc == 0) else []
                        pe = sch.add("tensor",
                                     _mm(ph[:, sl], asb[:, kc * 128:(kc + 1) * 128],
                                         lnt[:, kc * NT + t * TS: kc * NT + (t + 1) * TS],
                                         kc == 0, kc == KD - 1), waits=w0,
                                     inc="PE" if kc == KD - 1 else None) or pe
                    sch.add("vector", _ts1(hs[:, sl], ph[:, sl], self.coef[:, 0:1], ALU.mult),
                            waits=[("PE", pe)], inc="DVE")
                hs_done = sch.c("DVE")
            with nc.psum_tensor("pm0" + tag, [128, NT], F32) as pm0, \
                 nc.psum_tensor("pm1" + tag, [128, NT], F32) as pm1:
                wring = _Ring([wb0, wb1], g)
                xrr = _Ring([fx0, fx1], g)
                tsr = _Ring([ft0, ft1], g)
                osr = _Ring([fo0, fo1], g)
                pm_free = [dict(sch.psg()), dict(sch.psg())]
                for o in range(KD):
                    iw, wb, ww = wring.acquire()
                    wl = sch.add("gpsimd", _dma(wb[:], self.w_ffn[o, :, :]), waits=ww,
                                 inc="WL", inc_by=16)
                    ix, xrt, wx = xrr.acquire()
                    xl = sch.add("sync", _dma(xrt[:], x_src[o * 128:(o + 1) * 128, :]),
                                 waits=list(wx) + x_ready(o), inc="XL", inc_by=16)
                    pm = (pm0, pm1)[o % 2]
                    pe = 0
                    for t in range(NTS):
                        sl = slice(t * TS, (t + 1) * TS)
                        for kc in range(KD):
                            w0 = []
                            if t == 0 and kc == 0:
                                w0 = [("WL", wl), ("DVE", hs_done)] + list(pm_free[o % 2].items())
                            sch.add("tensor",
                                    _mm(pm[:, sl], wb[:, kc * 128:(kc + 1) * 128],
                                        lnt[:, kc * NT + t * TS: kc * NT + (t + 1) * TS],
                                        kc == 0, False), waits=w0)
                        wB_w = [("WL", wB)] if (o == 0 and t == 0) else []
                        pe = sch.add("tensor", _mm(pm[:, sl], bsb[:, o * 128:(o + 1) * 128],
                                                   hs[:, sl], False, True), waits=wB_w,
                                     inc="PE" if t == NTS - 1 else None) or pe
                    wring.release(iw, "PE", pe)
                    dv = 0
                    for hh in range(2):
                        slh = slice(hh * TH, (hh + 1) * TH)
                        it, tst, wt = tsr.acquire()
                        tv = sch.add("vector",
                                     lambda e, o_=tst[:], i_=pm[:, slh], s2=self.bhalf[:, o:o + 1]:
                                     e.tensor_scalar(out=o_, in0=i_, scalar1=0.5, scalar2=s2,
                                                     op0=ALU.mult, op1=ALU.add),
                                     waits=[("PE", pe)] + list(wt), inc="DVE")
                        io, ost, wo = osr.acquire()
                        dv = sch.add("vector", _tt(ost[:], tst[:], xrt[:, slh], ALU.add),
                                     waits=list(wo), inc="DVE")
                        tsr.release(it, "DVE", dv)
                        sv = sch.add("sync", _dma(out_hbm[o * 128:(o + 1) * 128,
                                                          hh * TH:(hh + 1) * TH], ost[:]),
                                     waits=[("DVE", dv)], inc="ST", inc_by=16)
                        osr.release(io, "ST", sv)
                        done[(o, hh)] = sv
                    xrr.release(ix, "DVE", dv)
                    pm_free[o % 2] = {"DVE": dv}
        return done

    def emit_s1(self, lnt, ln_done):
        nc, sch = self.nc, self.sch
        g = sch.guard()
        h_done = {}
        with contextlib.ExitStack() as es:
            sb = lambda n, s, d: es.enter_context(nc.sbuf_tensor(n, s, d))
            wg0 = sb("wg0", [128, D], BF16); wg1 = sb("wg1", [128, D], BF16)
            wu0 = sb("wu0", [128, D], BF16); wu1 = sb("wu1", [128, D], BF16)
            sg0 = sb("sg0", [128, TH], BF16); sg1 = sb("sg1", [128, TH], BF16)
            hst0 = sb("hst0", [128, NT], BF16); hst1 = sb("hst1", [128, NT], BF16)
            pg0 = es.enter_context(nc.psum_tensor("pg0", [128, TH], F32))
            pg1 = es.enter_context(nc.psum_tensor("pg1", [128, TH], F32))
            pu0 = es.enter_context(nc.psum_tensor("pu0", [128, TH], F32))
            pu1 = es.enter_context(nc.psum_tensor("pu1", [128, TH], F32))
            wgr = _Ring([wg0, wg1], g)
            wur = _Ring([wu0, wu1], g)
            sgr = _Ring([sg0, sg1], g)
            hr = _Ring([hst0, hst1], g)
            pg_free = [dict(sch.psg()), dict(sch.psg())]
            pu_free = [dict(sch.psg()), dict(sch.psg())]
            for f in range(KF):
                ig, wgt, wwg = wgr.acquire()
                wgl = sch.add("gpsimd", _dma(wgt[:], self.wg_t[f, :, :]), waits=wwg,
                              inc="WL", inc_by=16)
                iu, wut, wwu = wur.acquire()
                wul = sch.add("gpsimd", _dma(wut[:], self.wu_t[f, :, :]), waits=wwu,
                              inc="WL", inc_by=16)
                ih, hst, wh = hr.acquire()
                peg = peu = dv = 0
                for th in range(2):
                    pgi = (f * 2 + th) % 2
                    pg = (pg0, pg1)[pgi]
                    pu = (pu0, pu1)[pgi]
                    for ts in range(2):
                        for kc in range(KD):
                            w0 = []
                            if ts == 0 and kc == 0:
                                w0 = [("WL", wgl), ("DVE", ln_done)] + list(pg_free[pgi].items())
                            col = kc * NT + th * TH + ts * TS
                            peg = sch.add("tensor",
                                          _mm(pg[:, ts * TS:(ts + 1) * TS],
                                              wgt[:, kc * 128:(kc + 1) * 128],
                                              lnt[:, col:col + TS], kc == 0, kc == KD - 1),
                                          waits=w0,
                                          inc="PE" if (ts == 1 and kc == KD - 1) else None) or peg
                    for ts in range(2):
                        for kc in range(KD):
                            w0 = []
                            if ts == 0 and kc == 0:
                                w0 = [("WL", wul)] + list(pu_free[pgi].items())
                            col = kc * NT + th * TH + ts * TS
                            peu = sch.add("tensor",
                                          _mm(pu[:, ts * TS:(ts + 1) * TS],
                                              wut[:, kc * 128:(kc + 1) * 128],
                                              lnt[:, col:col + TS], kc == 0, kc == KD - 1),
                                          waits=w0,
                                          inc="PE" if (ts == 1 and kc == KD - 1) else None) or peu
                    isg, sgt, wsg = sgr.acquire()
                    av = sch.add("scalar", _act(sgt[:], pg[:], AF.Silu),
                                 waits=[("PE", peg)] + list(wsg), inc="ACT")
                    pg_free[pgi] = {"ACT": av}
                    dv = sch.add("vector", _tt(hst[:, th * TH:(th + 1) * TH], sgt[:], pu[:],
                                               ALU.mult),
                                 waits=[("ACT", av), ("PE", peu)] + (list(wh) if th == 0 else []),
                                 inc="DVE")
                    pu_free[pgi] = {"DVE": dv}
                    sgr.release(isg, "DVE", dv)
                wgr.release(ig, "PE", peg)
                wur.release(iu, "PE", peu)
                sv = sch.add("sync", _dma(self.h_hbm[f * 128:(f + 1) * 128, :], hst[:]),
                             waits=[("DVE", dv)], inc="ST", inc_by=16)
                hr.release(ih, "ST", sv)
                h_done[f] = sv
        return h_done

    def emit_s2(self, xa_done, h_done, dst):
        nc, sch = self.nc, self.sch
        g = sch.guard()
        xm_done = {}
        with contextlib.ExitStack() as es:
            sb = lambda n, s, d: es.enter_context(nc.sbuf_tensor(n, s, d))
            hbuf = sb("hbuf", [128, KF * TH], BF16)
            wd0 = sb("wd0", [128, FF // 2], BF16); wd1 = sb("wd1", [128, FF // 2], BF16)
            wd2 = sb("wd2", [128, FF // 2], BF16)
            sx0b = sb("sx0b", [128, TH], F32); sx1b = sb("sx1b", [128, TH], F32)
            so0 = sb("so0", [128, TH], F32); so1 = sb("so1", [128, TH], F32)
            spm0 = es.enter_context(nc.psum_tensor("spm0", [128, TH], F32))
            spm1 = es.enter_context(nc.psum_tensor("spm1", [128, TH], F32))
            xrr = _Ring([sx0b, sx1b], g)
            osr = _Ring([so0, so1], g)
            wdr = _Ring([wd0, wd1, wd2], g)
            pm_free = [dict(sch.psg()), dict(sch.psg())]
            last_pe = 0
            for th in range(2):
                hl = []
                for grp in range(6):
                    f0 = grp * 13
                    w0 = [("ST", h_done[f0 + 12])]
                    w0 += list(g) if th == 0 else [("PE", last_pe)]
                    hv = sch.add(
                        "sync",
                        _dma(hbuf[:, f0 * TH:(f0 + 13) * TH].rearrange("p (f t) -> p f t", t=TH),
                             self.h_hbm[f0 * 128:(f0 + 13) * 128,
                                        th * TH:(th + 1) * TH].rearrange("(f p) t -> p f t", p=128)),
                        waits=w0, inc="XL", inc_by=16)
                    hl.append(hv)
                for o in range(KD):
                    i0, wdt0, ww0 = wdr.acquire()
                    w0l = sch.add("gpsimd", _dma(wdt0[:], self.wd_t[o, :, 0:FF // 2]),
                                  waits=ww0, inc="WL", inc_by=16)
                    i1, wdt1, ww1 = wdr.acquire()
                    w1l = sch.add("gpsimd", _dma(wdt1[:], self.wd_t[o, :, FF // 2:FF]),
                                  waits=ww1, inc="WL", inc_by=16)
                    ix, xrt, wx = xrr.acquire()
                    xl = sch.add("sync", _dma(xrt[:], self.xa[o * 128:(o + 1) * 128,
                                                             th * TH:(th + 1) * TH]),
                                 waits=list(wx) + [("ST", xa_done[(o, th)])],
                                 inc="XL", inc_by=16)
                    pmi = o % 2
                    pm = (spm0, spm1)[pmi]
                    pe = 0
                    for ts in range(2):
                        for kc in range(KF):
                            w0 = []
                            if ts == 0 and kc == 0:
                                w0 = [("WL", w0l)] + list(pm_free[pmi].items())
                            if kc == 39 and ts == 0:
                                w0 = [("WL", w1l)]
                            if o == 0 and ts == 0 and kc % 13 == 0:
                                w0 = w0 + [("XL", hl[kc // 13])]
                            wdt = (wdt0, wdt1)[kc // 39]
                            jj = kc % 39
                            inc = None
                            if ts == 1 and kc in (38, KF - 1):
                                inc = "PE"
                            pe = sch.add("tensor",
                                         _mm(pm[:, ts * TS:(ts + 1) * TS],
                                             wdt[:, jj * 128:(jj + 1) * 128],
                                             hbuf[:, kc * TH + ts * TS: kc * TH + (ts + 1) * TS],
                                             kc == 0, kc == KF - 1), waits=w0,
                                         inc=inc) or pe
                            if ts == 1 and kc == 38:
                                wdr.release(i0, "PE", pe)
                    wdr.release(i1, "PE", pe)
                    io, ost, wo = osr.acquire()
                    dv = sch.add("vector", _stt(ost[:], pm[:], 1.0, xrt[:], ALU.mult, ALU.add),
                                 waits=[("PE", pe), ("XL", xl)] + list(wo), inc="DVE")
                    pm_free[pmi] = {"DVE": dv}
                    xrr.release(ix, "DVE", dv)
                    sv = sch.add("sync", _dma(dst[o * 128:(o + 1) * 128,
                                                  th * TH:(th + 1) * TH], ost[:]),
                                 waits=[("DVE", dv)], inc="ST", inc_by=16)
                    osr.release(io, "ST", sv)
                    xm_done[(o, th)] = sv
                last_pe = sch.c("PE")
        return xm_done

    def emit_approx_block(self, b, ln2_st, xm_done, dst, xb_done):
        nc, sch = self.nc, self.sch
        tagb = f"X{b}"
        with contextlib.ExitStack() as eo:
            au = eo.enter_context(nc.sbuf_tensor("au" + tagb, [128, KS * TH], BF16))
            dl = eo.enter_context(nc.sbuf_tensor("dl" + tagb, [128, KS * TH], F32))
            self._approx_updelta(b, tagb, au, dl, ln2_st)
            with contextlib.ExitStack() as et:
                tm = [eo if False else et.enter_context(
                    nc.sbuf_tensor(f"tm{m}{tagb}", [128, KS * TH], BF16)) for m in range(1, 7)]
                auk = et.enter_context(nc.sbuf_tensor("auk" + tagb, [128, KS * TH], BF16))
                tmps_done = self._approx_tmps(b, tagb, au, dl, tm, auk)
                self._approx_mms(b, tagb, [au] + tm, tmps_done, xm_done, dst, xb_done)

    def _approx_updelta(self, b, tagb, au, dl, ln2_st):
        nc, sch = self.nc, self.sch
        g = sch.guard()
        with contextlib.ExitStack() as es:
            sb = lambda n, s, d: es.enter_context(nc.sbuf_tensor(n + tagb, s, d))
            lnb = sb("lnb", [128, KD * TH], BF16)
            ub0 = sb("ub0", [128, D], BF16); ub1 = sb("ub1", [128, D], BF16)
            gb0 = sb("gb0", [128, D], BF16); gb1 = sb("gb1", [128, D], BF16)
            pau0 = es.enter_context(nc.psum_tensor("pau0" + tagb, [128, TH], F32))
            pau1 = es.enter_context(nc.psum_tensor("pau1" + tagb, [128, TH], F32))
            pdl0 = es.enter_context(nc.psum_tensor("pdl0" + tagb, [128, TH], F32))
            pdl1 = es.enter_context(nc.psum_tensor("pdl1" + tagb, [128, TH], F32))
            lnl = 0
            for grp in range(4):
                k0 = grp * 8
                lnl = sch.add(
                    "sync",
                    _dma(lnb[:, k0 * TH:(k0 + 8) * TH].rearrange("p (k t) -> p k t", t=TH),
                         self.ln2s[k0 * 128:(k0 + 8) * 128,
                                   b * TH:(b + 1) * TH].rearrange("(k p) t -> p k t", p=128)),
                    waits=list(g) + [("ST", ln2_st[k0 + 7])], inc="XL", inc_by=16)
            ur = _Ring([ub0, ub1], g)
            gr = _Ring([gb0, gb1], g)
            pau_free = [dict(sch.psg()), dict(sch.psg())]
            pdl_free = [dict(sch.psg()), dict(sch.psg())]
            for os_ in range(KS):
                iu, ubt, wwu = ur.acquire()
                ul = sch.add("gpsimd", _dma(ubt[:], self.up_t[os_, :, :]), waits=wwu,
                             inc="WL", inc_by=16)
                ig, gbt, wwg = gr.acquire()
                gl = sch.add("gpsimd", _dma(gbt[:], self.gate_t[os_, :, :]), waits=wwg,
                             inc="WL", inc_by=16)
                pi = os_ % 2
                pau = (pau0, pau1)[pi]
                pdl = (pdl0, pdl1)[pi]
                peu = peg = 0
                for ts in range(2):
                    for kc in range(KD):
                        w0 = []
                        if ts == 0 and kc == 0:
                            w0 = [("WL", ul), ("XL", lnl)] + list(pau_free[pi].items())
                        peu = sch.add("tensor",
                                      _mm(pau[:, ts * TS:(ts + 1) * TS],
                                          ubt[:, kc * 128:(kc + 1) * 128],
                                          lnb[:, kc * TH + ts * TS: kc * TH + (ts + 1) * TS],
                                          kc == 0, kc == KD - 1), waits=w0,
                                      inc="PE" if (ts == 1 and kc == KD - 1) else None) or peu
                av_u = sch.add("scalar", _act(au[:, os_ * TH:(os_ + 1) * TH], pau[:], AF.Copy),
                               waits=[("PE", peu)], inc="ACT")
                pau_free[pi] = {"ACT": av_u}
                for ts in range(2):
                    for kc in range(KD):
                        w0 = []
                        if ts == 0 and kc == 0:
                            w0 = [("WL", gl)] + list(pdl_free[pi].items())
                        peg = sch.add("tensor",
                                      _mm(pdl[:, ts * TS:(ts + 1) * TS],
                                          gbt[:, kc * 128:(kc + 1) * 128],
                                          lnb[:, kc * TH + ts * TS: kc * TH + (ts + 1) * TS],
                                          kc == 0, kc == KD - 1), waits=w0,
                                      inc="PE" if (ts == 1 and kc == KD - 1) else None) or peg
                dv_d = sch.add("vector", _ts1(dl[:, os_ * TH:(os_ + 1) * TH], pdl[:],
                                              self.neglp[:, os_:os_ + 1], ALU.add),
                               waits=[("PE", peg)], inc="DVE")
                pdl_free[pi] = {"DVE": dv_d}
                ur.release(iu, "PE", peu)
                gr.release(ig, "PE", peg)

    def _approx_tmps(self, b, tagb, au, dl, tm, auk):
        nc, sch = self.nc, self.sch
        with contextlib.ExitStack() as es:
            sb = lambda n: es.enter_context(nc.sbuf_tensor(n + tagb, [128, TH], F32))
            absd = sb("absd"); logd = sb("logd"); keep = sb("keep")
            dta = sb("dta"); dtb = sb("dtb"); uu = sb("uu")
            dts = (dta, dtb)
            dt_free = {0: 0, 1: 0}
            for kc in range(KS):
                sl = slice(kc * TH, (kc + 1) * TH)
                sch.add("scalar", _act(absd[:], dl[:, sl], AF.Abs), inc="ACT")
                a_log = sch.add("scalar", _act(logd[:], absd[:], AF.Ln), inc="ACT")
                sch.add("vector", _ts1(keep[:], absd[:], THD, ALU.is_le),
                        waits=[("ACT", a_log)], inc="DVE")
                sch.add("vector", _tt(auk[:, sl], au[:, sl], keep[:], ALU.mult), inc="DVE")
                sch.add("vector", _tt(tm[0][:, sl], dl[:, sl], au[:, sl], ALU.mult), inc="DVE")
                for m in range(2, 7):
                    dt = dts[m % 2]
                    a_dt = sch.add("scalar",
                                   _act(dt[:], logd[:], AF.Exp, bias=self.negdc[:, m - 1:m],
                                        scale=float(m if m % 2 == 0 else m - 1)),
                                   waits=[("DVE", dt_free[m % 2])], inc="ACT")
                    src2 = auk if m > 4 else au
                    if m % 2 == 0:
                        dvv = sch.add("vector", _tt(tm[m - 1][:, sl], dt[:], src2[:, sl], ALU.mult),
                                      waits=[("ACT", a_dt)], inc="DVE")
                    else:
                        sch.add("vector", _tt(uu[:], dt[:], dl[:, sl], ALU.mult),
                                waits=[("ACT", a_dt)], inc="DVE")
                        dvv = sch.add("vector", _tt(tm[m - 1][:, sl], uu[:], src2[:, sl], ALU.mult),
                                      inc="DVE")
                    dt_free[m % 2] = dvv
            return sch.c("DVE")

    def _approx_mms(self, b, tagb, tmps, tmps_done, xm_done, dst, xb_done):
        nc, sch = self.nc, self.sch
        g2 = sch.guard()
        with contextlib.ExitStack() as es:
            sb = lambda n, s, d: es.enter_context(nc.sbuf_tensor(n + tagb, s, d))
            awb0 = sb("awb0", [128, 7 * SS], BF16); awb1 = sb("awb1", [128, 7 * SS], BF16)
            xmr0 = sb("xmr0", [128, TH], F32); xmr1 = sb("xmr1", [128, TH], F32)
            ao0 = sb("ao0", [128, TH], F32); ao1 = sb("ao1", [128, TH], F32)
            apm0 = es.enter_context(nc.psum_tensor("apm0" + tagb, [128, TH], F32))
            apm1 = es.enter_context(nc.psum_tensor("apm1" + tagb, [128, TH], F32))
            awr = _Ring([awb0, awb1], g2)
            xrr = _Ring([xmr0, xmr1], g2)
            osr = _Ring([ao0, ao1], g2)
            pm_free = [dict(sch.psg()), dict(sch.psg())]
            for o in range(KD):
                ia, awt, wa = awr.acquire()
                awl = sch.add("gpsimd",
                              _dma(awt[:].rearrange("p (m k) -> p m k", k=SS),
                                   self.aw_t[o, :, :, :].rearrange("m p k -> p m k")),
                              waits=wa, inc="WL", inc_by=16)
                ix, xrt, wx = xrr.acquire()
                xl = sch.add("sync", _dma(xrt[:], self.xm[o * 128:(o + 1) * 128,
                                                          b * TH:(b + 1) * TH]),
                             waits=list(wx) + [("ST", xm_done[(o, b)])], inc="XL", inc_by=16)
                pmi = o % 2
                pm = (apm0, apm1)[pmi]
                pe = 0
                for ts in range(2):
                    for m in range(7):
                        for kc in range(KS):
                            w0 = []
                            if ts == 0 and m == 0 and kc == 0:
                                w0 = [("WL", awl), ("DVE", tmps_done)] + list(pm_free[pmi].items())
                            pe = sch.add(
                                "tensor",
                                _mm(pm[:, ts * TS:(ts + 1) * TS],
                                    awt[:, m * SS + kc * 128: m * SS + (kc + 1) * 128],
                                    tmps[m][:, kc * TH + ts * TS: kc * TH + (ts + 1) * TS],
                                    (m == 0 and kc == 0), (m == 6 and kc == KS - 1)),
                                waits=w0,
                                inc="PE" if (ts == 1 and m == 6 and kc == KS - 1) else None) or pe
                awr.release(ia, "PE", pe)
                io, ost, wo = osr.acquire()
                dv = sch.add("vector", _stt(ost[:], pm[:], 1.0, xrt[:], ALU.mult, ALU.add),
                             waits=[("PE", pe), ("XL", xl)] + list(wo), inc="DVE")
                pm_free[pmi] = {"DVE": dv}
                xrr.release(ix, "DVE", dv)
                sv = sch.add("sync", _dma(dst[o * 128:(o + 1) * 128, b * TH:(b + 1) * TH],
                                          ost[:]),
                             waits=[("DVE", dv)], inc="ST", inc_by=16)
                osr.release(io, "ST", sv)
                xb_done[(o, b)] = sv

    def build(self):
        nc, sch, stage = self.nc, self.sch, self.stage
        ffn1_dst = self.xa if stage > 1 else self.y
        s2_dst = self.xm if stage > 2 else self.y
        ap_dst = self.xbs if stage > 3 else self.y
        with contextlib.ExitStack() as es:
            sems = {n: es.enter_context(nc.semaphore(n))
                    for n in ("WL", "XL", "ST", "PE", "ACT", "DVE")}
            self.emit_consts(es)

            with contextlib.ExitStack() as e1:
                lnA = e1.enter_context(nc.sbuf_tensor("lnA", [128, KD * NT], BF16))
                ln1_done, _ = self.emit_ln(self.x_in, lambda kc: [], lnA, tag="L1")
                if stage == 0:
                    for kc in range(KD):
                        sch.add("gpsimd", _dma(self.y[kc * 128:(kc + 1) * 128, :],
                                               lnA[:, kc * NT:(kc + 1) * NT]),
                                waits=[("DVE", ln1_done)], inc="ST", inc_by=16)
                    xa_done = {}
                else:
                    xa_done = self.emit_ffn(lnA, ln1_done, self.x_in, lambda o: [],
                                            ffn1_dst, tag="F1")

            if stage >= 2:
                with contextlib.ExitStack() as e2:
                    lnB = e2.enter_context(nc.sbuf_tensor("lnB", [128, KD * NT], BF16))
                    ln2_done, ln2_st = self.emit_ln(
                        self.xa, lambda kc: [("ST", xa_done[(kc, 1)])], lnB,
                        store=self.ln2s, tag="L2")
                    h_done = self.emit_s1(lnB, ln2_done)
                xm_done = self.emit_s2(xa_done, h_done, s2_dst)

            if stage >= 3:
                xb_done = {}
                for b in range(2):
                    self.emit_approx_block(b, ln2_st, xm_done, ap_dst, xb_done)

            if stage >= 4:
                with contextlib.ExitStack() as e4:
                    lnC = e4.enter_context(nc.sbuf_tensor("lnC", [128, KD * NT], BF16))
                    ln3_done, _ = self.emit_ln(self.xbs,
                                               lambda kc: [("ST", xb_done[(kc, 1)])],
                                               lnC, tag="L3")
                    self.emit_ffn(lnC, ln3_done, self.xbs,
                                  lambda o: [("ST", xb_done[(o, 1)])], self.y,
                                  tag="F2")

            sch.add("sync", None, waits=[("ST", sch.c("ST"))])
            with nc.Block() as block:
                sch.run(block, sems)
        return nc


def build_nc(stage=4):
    return _Builder(stage).build()


def _tile_lhsT(W):
    W = np.asarray(W, np.float32)
    dout, din = W.shape
    nk, no = din // 128, dout // 128
    return np.ascontiguousarray(
        W.T.reshape(nk, 128, no, 128).transpose(2, 1, 0, 3).reshape(no, 128, nk * 128))


def prep_inputs(inp):
    d = {k: np.asarray(v, np.float32) for k, v in inp.items()}
    x = np.ascontiguousarray(d["L0_attn_hidden_states"].reshape(NCORES * NT, D))
    shards = [np.ascontiguousarray(x[c * NT:(c + 1) * NT].T) for c in range(NCORES)]
    W = {}
    W["w_ffn"] = _tile_lhsT(d["L0_ffn_W"])
    W["a_cat"] = _tile_lhsT(d["L0_ffn_lora_A"].reshape(128, D))[0]
    W["b_cat"] = np.ascontiguousarray(d["L0_ffn_lora_B"].transpose(0, 2, 1).reshape(128, D))
    W["bias_t"] = np.ascontiguousarray(d["L0_ffn_bias"].reshape(KD, 128).T)
    W["lw_rep"] = np.ascontiguousarray(np.repeat(d["L0_ffn_lora_weights"], 16).reshape(128, 1))
    W["up_t"] = _tile_lhsT(d["L0_attn_up_w"])
    W["gate_t"] = _tile_lhsT(d["L0_attn_gate_w"])
    W["lp_t"] = np.ascontiguousarray(d["L0_attn_local_point"].reshape(KS, 128).T)
    W["disc_t"] = np.ascontiguousarray(d["L0_attn_discount_factor"].reshape(1, NG))
    la = _tile_lhsT(d["L0_attn_local_approx_output"])
    fu = [_tile_lhsT(np.ascontiguousarray(d["L0_attn_fuse_weight"][:, :, i])) for i in range(NG)]
    W["aw_t"] = np.ascontiguousarray(np.stack([la] + fu, axis=1))
    W["wg_t"] = _tile_lhsT(d["L0_attn_ffn_gate_w"])
    W["wu_t"] = _tile_lhsT(d["L0_attn_ffn_up_w"])
    W["wd_t"] = _tile_lhsT(d["L0_attn_ffn_down_w"])
    return W, shards


def kernel(**inp):
    W, shards = prep_inputs(inp)
    nc = build_nc(4)
    in_maps = [dict(W, x_in=shards[c]) for c in range(NCORES)]
    res = run_bass_kernel_spmd(nc, in_maps, core_ids=list(range(NCORES)))
    out = np.stack([np.asarray(res.results[c]["y"]) for c in range(NCORES)])
    out = out.transpose(0, 2, 1).reshape(2, NCORES * NT // 2, D)
    return np.ascontiguousarray(out)
